# version 12
# baseline (speedup 1.0000x reference)
import numpy as np
import ml_dtypes
import concourse.bass as bass
import concourse.mybir as mybir
from concourse.bass_utils import run_bass_kernel_spmd

F32 = mybir.dt.float32
BF16 = mybir.dt.bfloat16
AF = mybir.ActivationFunctionType
ALU = mybir.AluOpType

D = 2048
S = 8192
NH = 32
DH = 64
HID = 8192
W = 8
HPC = 4
TPC = S // W
NTB = 16
NDC = 16
NHC = 64
KROW = 64 + 9
EPS = 1e-5
TOTKB = sum(4 * qc + 4 for qc in range(NTB))


def _slope(h):
    return 2.0 ** (-(h + 1) * 8.0 / NH)


def _bf16_split3(x):
    x = np.asarray(x, dtype=np.float64)
    a = x.astype(ml_dtypes.bfloat16)
    r1 = x - a.astype(np.float64)
    b = r1.astype(ml_dtypes.bfloat16)
    r2 = r1 - b.astype(np.float64)
    c = r2.astype(ml_dtypes.bfloat16)
    return a, b, c


def kbidx(qc, kb):
    return 2 * qc * (qc + 1) + kb


def host_prep(inputs):
    x = np.ascontiguousarray(np.asarray(inputs["L0_attn_hidden_states"], dtype=np.float32).reshape(S, D))
    Wq = np.asarray(inputs["L0_attn_Wq"], dtype=np.float32)
    Wk = np.asarray(inputs["L0_attn_Wk"], dtype=np.float32)
    Wv = np.asarray(inputs["L0_attn_Wv"], dtype=np.float32)
    Wo = np.asarray(inputs["L0_attn_Wo"], dtype=np.float32)
    bo = np.asarray(inputs["L0_attn_bo"], dtype=np.float32)
    Wg = np.asarray(inputs["L0_ffn_w_gate"], dtype=np.float32)
    Wu = np.asarray(inputs["L0_ffn_w_up"], dtype=np.float32)
    Wd = np.asarray(inputs["L0_ffn_w_down"], dtype=np.float32)

    x_fm = np.ascontiguousarray(x.T)
    shared = {
        "x_fm": x_fm,
        "wo": np.ascontiguousarray(Wo),
        "wg_T": np.ascontiguousarray(Wg.T),
        "wu_T": np.ascontiguousarray(Wu.T),
        "wd_T": np.ascontiguousarray(Wd.T),
        "bo_cols": np.ascontiguousarray(bo.reshape(NDC, 128).T),
    }

    jj = np.arange(32, dtype=np.float64)
    invf = 10000.0 ** (-jj / 32.0)
    t = np.arange(S, dtype=np.float64)
    ang = t[None, :] * invf[:, None]
    shared["cos128"] = np.ascontiguousarray(np.tile(np.cos(ang), (4, 1)).astype(np.float32))
    shared["sin128"] = np.ascontiguousarray(np.tile(np.sin(ang), (4, 1)).astype(np.float32))

    pos = np.arange(S, dtype=np.float64)
    jh = pos // 64
    jl = pos % 64
    shared["kaux"] = np.ascontiguousarray(np.stack([jh, jl, jh, jl, jh, jl], 0).astype(ml_dtypes.bfloat16))

    r = np.arange(128)[:, None]
    c = np.arange(512)[None, :]
    shared["mask4"] = np.ascontiguousarray(
        np.concatenate([(r <= c - 128 * a) for a in range(4)], axis=1).astype(ml_dtypes.bfloat16))

    per_core = []
    for core in range(W):
        d = {}
        heads = [HPC * core + i for i in range(HPC)]
        cols_ev = [h * DH + 2 * k for h in heads for k in range(32)]
        cols_od = [h * DH + 2 * k + 1 for h in heads for k in range(32)]
        d["wq_p"] = np.ascontiguousarray(Wq[:, cols_ev + cols_od])
        kev = [core * DH + 2 * k for k in range(32)]
        kod = [core * DH + 2 * k + 1 for k in range(32)]
        d["wk_p"] = np.ascontiguousarray(Wk[:, kev + kod])
        d["wv_p"] = np.ascontiguousarray(Wv[:, core * DH:(core + 1) * DH])
        d["x_dense"] = np.ascontiguousarray(x_fm[:, core * TPC:(core + 1) * TPC])

        qa = np.zeros((9, HPC * S), dtype=ml_dtypes.bfloat16)
        for i, h in enumerate(heads):
            sl = _slope(h)
            s1, s2, s3 = _bf16_split3(np.array([sl]))
            consts = [64.0 * float(s1[0]), float(s1[0]), 64.0 * float(s2[0]),
                      float(s2[0]), 64.0 * float(s3[0]), float(s3[0])]
            for rr in range(6):
                qa[rr, i * S:(i + 1) * S] = np.float32(consts[rr])
            v1, v2, v3 = _bf16_split3(-sl * pos)
            qa[6, i * S:(i + 1) * S] = v1
            qa[7, i * S:(i + 1) * S] = v2
            qa[8, i * S:(i + 1) * S] = v3
        d["qaux2"] = qa
        per_core.append(d)
    return shared, per_core


class Ch:

    def __init__(self, nc, stack, name, step):
        self.sem = stack.enter_context(nc.semaphore(name))
        self.step = step
        self.n = 0

    def inc(self, inst):
        inst.then_inc(self.sem, self.step)
        self.n += 1
        return self.n * self.step

    def at(self, i):
        return (i + 1) * self.step

    def wait(self, eng, i):
        if i >= 0:
            eng.wait_ge(self.sem, self.at(i))

    def wait_cur(self, eng):
        eng.wait_ge(self.sem, self.n * self.step)


def build(debug=None):
    nc = bass.Bass("TRN2", target_bir_lowering=False, detect_race_conditions=False, num_devices=8)
    I, O = "ExternalInput", "ExternalOutput"
    ap = lambda name, shape, dt, kind: nc.dram_tensor(name, shape, dt, kind=kind).ap()
    x_fm = ap("x_fm", [D, S], F32, I)
    wq_p = ap("wq_p", [D, 256], F32, I)
    wk_p = ap("wk_p", [D, 64], F32, I)
    wv_p = ap("wv_p", [D, 64], F32, I)
    wo = ap("wo", [D, D], F32, I)
    wg_T = ap("wg_T", [D, HID], F32, I)
    wu_T = ap("wu_T", [D, HID], F32, I)
    wd_T = ap("wd_T", [HID, D], F32, I)
    bo_cols = ap("bo_cols", [128, NDC], F32, I)
    cos128 = ap("cos128", [128, S], F32, I)
    sin128 = ap("sin128", [128, S], F32, I)
    kaux = ap("kaux", [6, S], BF16, I)
    mask4 = ap("mask4", [128, 2048], BF16, I)
    qaux2 = ap("qaux2", [9, HPC * S], BF16, I)
    x_dense = ap("x_dense", [D, TPC], F32, I)
    out_fm = ap("out_fm", [D, TPC], F32, O)

    q_dram = nc.dram_tensor("q_dram", [KROW, HPC * S], BF16).ap()
    ctx_loc = nc.dram_tensor("ctx_loc", [HPC * DH, S], BF16).ap()
    ctx_full = nc.dram_tensor("ctx_full", [D, S], BF16, addr_space="Shared").ap()
    flag_in = nc.dram_tensor("flag_in", [1, 8], F32).ap()
    flag_out = nc.dram_tensor("flag_out", [W, 8], F32, addr_space="Shared").ap()

    dbg = {}
    if debug == "p1":
        dbg["k"] = ap("dbg_k", [KROW, S], F32, O)
        dbg["v"] = ap("dbg_v", [128, 64 * 65], F32, O)
        dbg["q"] = ap("dbg_q", [KROW, HPC * S], BF16, O)
    if debug == "attn":
        dbg["ctx"] = ap("dbg_ctx", [D, S], BF16, O)
    if debug == "attn0":
        dbg["ps"] = ap("dbg_ps", [65, 2048], F32, O)
    if debug == "d0":
        dbg["ctxf"] = ap("dbg_ctxf", [128, NDC * 512], BF16, O)
        dbg["rb"] = ap("dbg_rb", [128, NDC * 512], F32, O)
        dbg["x2"] = ap("dbg_x2", [128, NDC * 512], F32, O)
        dbg["hid"] = ap("dbg_hid", [128, NHC * 512], BF16, O)
        dbg["r2"] = ap("dbg_r2", [128, NDC * 512], F32, O)
    if debug == "attn1":
        dbg["k"] = ap("dbg_k", [KROW, S], F32, O)
        dbg["ctxl"] = ap("dbg_ctxl", [HPC * DH, S], BF16, O)
        dbg["den0"] = ap("dbg_den0", [1, 2048], F32, O)
        dbg["qsbraw"] = ap("dbg_qsbraw", [KROW, 2048], BF16, O)
        dbg["qdr"] = ap("dbg_qdr", [KROW, 2048], BF16, O)
        dbg["qsb0"] = ap("dbg_qsb0", [128, 2048], BF16, O)
        dbg["ps0"] = ap("dbg_ps0", [65, 2048], BF16, O)


    import contextlib
    stk = contextlib.ExitStack()
    C = {}
    for nm in ["BO", "WQKV", "KAUX", "CONSTS", "P1X", "COS", "KM", "VM", "QM",
               "KR", "VE", "QST", "QD",
               "QSB", "SC", "CT", "TH", "EX", "MK", "RC", "BC", "BE", "CM", "CS", "CC",
               "DC", "P2", "CTXF", "XRES", "WODMA", "WOM", "REP",
               "LNC", "LNQ", "LNM", "LNR", "LNA", "LNB", "LNE", "APP",
               "WG", "GU", "SI", "HD", "WD", "DN", "R2", "ST", "DBG"]:
        step = 16 if nm in ("BO", "WQKV", "KAUX", "P1X", "COS", "QD", "QSB", "CS",
                            "CTXF", "XRES", "WODMA", "WG", "WD", "ST", "DBG") else 1
        C[nm] = None, step, nm
    with stk:
        for nm in list(C):
            _, step, _ = C[nm]
            C[nm] = Ch(nc, stk, nm, step)

        consts = stk.enter_context(nc.sbuf_tensor("consts", [128, 4], F32))
        ones1 = stk.enter_context(nc.sbuf_tensor("ones1", [1, 128], F32))
        onescol_bf = stk.enter_context(nc.sbuf_tensor("onescol_bf", [128, 1], BF16))
        ones65_bf = stk.enter_context(nc.sbuf_tensor("ones65_bf", [65, 128], BF16))
        bo_sb = stk.enter_context(nc.sbuf_tensor("bo_sb", [128, NDC], F32))
        wo_sb = stk.enter_context(nc.sbuf_tensor("wo_sb", [128, 2 * 2048], BF16))
        ps_a = stk.enter_context(nc.psum_tensor("ps_a", [128, 2048], F32))
        ps_b = stk.enter_context(nc.psum_tensor("ps_b", [128, 2048], F32))
        block = stk.enter_context(nc.Block())
        if True:

            nc.const_aps.register(F32, -30.0, consts[:, 0:1])
            nc.const_aps.register(F32, EPS, consts[:, 1:2])
            nc.const_aps.register(F32, -60.0, consts[:, 2:3])


            @block.vector
            def _(v):
                v.memset(consts[:, 0:1], -30.0)
                v.memset(consts[:, 1:2], EPS)
                v.memset(consts[:, 2:3], -60.0)
                v.memset(ones1[:, :], 1.0)
                v.memset(ones65_bf[:, :], 1.0)
                C["CONSTS"].inc(v.memset(onescol_bf[:, :], 1.0))

            @block.scalar
            def _(a):
                a.activation(out=consts[:, 3:4], in_=consts[:, 0:1], func=AF.Tanh)
                a.activation(out=consts[:, 3:4], in_=consts[:, 0:1], func=AF.Exp)

            @block.sync
            def _(s):
                C["BO"].inc(s.dma_start(out=bo_sb[:, :], in_=bo_cols[:, :]))

            import contextlib as _ctl
            es1 = _ctl.ExitStack()
            wq_sb = es1.enter_context(nc.sbuf_tensor("wq_sb", [128, NDC * 256], BF16))
            wk_sb = es1.enter_context(nc.sbuf_tensor("wk_sb", [128, NDC * 64], BF16))
            wv_sb = es1.enter_context(nc.sbuf_tensor("wv_sb", [128, NDC * 64], BF16))
            k_sb = es1.enter_context(nc.sbuf_tensor("k_sb", [128, S], BF16))
            v_sb = es1.enter_context(nc.sbuf_tensor("v_sb", [128, 64 * 65], BF16))
            mask_sb = es1.enter_context(nc.sbuf_tensor("mask_sb", [128, 2048], BF16))
            x_tile = es1.enter_context(nc.sbuf_tensor("x_tile", [128, 2 * S], BF16))
            cos_t = es1.enter_context(nc.sbuf_tensor("cos_t", [128, 2 * 512], F32))
            sin_t = es1.enter_context(nc.sbuf_tensor("sin_t", [128, 2 * 512], F32))
            krs = es1.enter_context(nc.sbuf_tensor("krs", [32, 2 * 512], F32))
            qrs = es1.enter_context(nc.sbuf_tensor("qrs", [128, 2 * 512], F32))
            qstage = es1.enter_context(nc.sbuf_tensor("qstage", [64, 2 * 2048], BF16))
            q_sb = es1.enter_context(nc.sbuf_tensor("q_sb", [128, 2 * 2048], BF16))
            tanh_r = es1.enter_context(nc.sbuf_tensor("tanh_r", [128, 2 * 2048], F32))
            probs_r = es1.enter_context(nc.sbuf_tensor("probs_r", [128, 4 * 2048], BF16))
            tm_r = es1.enter_context(nc.sbuf_tensor("tm_r", [128, 2 * 2048], F32))
            den_r = es1.enter_context(nc.sbuf_tensor("den_r", [1, 512], F32))
            denr_all = es1.enter_context(nc.sbuf_tensor("denr_all", [1, 2048], F32))
            dr3 = es1.enter_context(nc.sbuf_tensor("dr3", [33, 2048], BF16))
            drs = es1.enter_context(nc.sbuf_tensor("drs", [1, 2048], F32))
            den_s = es1.enter_context(nc.sbuf_tensor("den_s", [1, 2048], F32))
            if debug == "attn1":
                den_keep = es1.enter_context(nc.sbuf_tensor("den_keep", [1, 2048], F32))
                qsb_keep = es1.enter_context(nc.sbuf_tensor("qsb_keep", [128, 2048], BF16))
                ps_keep = es1.enter_context(nc.sbuf_tensor("ps_keep", [65, 2048], BF16))
            den_bc = es1.enter_context(nc.sbuf_tensor("den_bc", [64, 512], F32))
            cstage = es1.enter_context(nc.sbuf_tensor("cstage", [64, 8 * 512], BF16))
            if True:

                kps = lambda b: ps_a[0:64, b * 512:(b + 1) * 512]
                qevp = lambda b: ps_a[:, (2 + b) * 512:(3 + b) * 512]
                qodp = lambda b: ps_b[:, b * 512:(b + 1) * 512]
                vps = lambda b: ps_b[:, (2 + b) * 512:(2 + b) * 512 + 256]

                @block.gpsimd
                def _(g):
                    C["WQKV"].inc(g.dma_start(
                        out=wq_sb[:].rearrange("p (c n) -> p c n", n=256),
                        in_=wq_p.rearrange("(c p) n -> p c n", p=128)))
                    C["WQKV"].inc(g.dma_start(
                        out=wk_sb[:].rearrange("p (c n) -> p c n", n=64),
                        in_=wk_p.rearrange("(c p) n -> p c n", p=128)))
                    C["WQKV"].inc(g.dma_start(
                        out=wv_sb[:].rearrange("p (c n) -> p c n", n=64),
                        in_=wv_p.rearrange("(c p) n -> p c n", p=128)))
                    for tb in range(NTB):
                        if tb >= 2:
                            C["QM"].wait(g, tb - 2)
                        C["P1X"].inc(g.dma_start(
                            out=x_tile[:, (tb % 2) * S:(tb % 2) * S + S].rearrange(
                                "p (c n) -> p c n", n=512),
                            in_=x_fm.rearrange("(c p) t -> p c t", p=128)[:, :, tb * 512:(tb + 1) * 512]))

                @block.vector
                def _(v):
                    v.memset(dr3[:, :], 0.0)
                    v.memset(k_sb[64:128, :], 0.0)
                    v.memset(q_sb[64:128, :], 0.0)
                    v.memset(k_sb[64:KROW, :], 1.0)
                    C["CONSTS"].inc(v.memset(
                        v_sb[:].rearrange("p (k c) -> p k c", c=65)[:, :, 64:65], 1.0))

                def _qd_store(s, tb):
                    b = tb % 2
                    C["QST"].wait(s, tb)
                    C["QD"].inc(s.dma_start(
                        out=q_dram[0:64, :].rearrange("r (h t) -> r h t", h=HPC)[:, :, tb * 512:(tb + 1) * 512],
                        in_=qstage[:, b * 2048:(b + 1) * 2048].rearrange("r (h t) -> r h t", h=HPC)))

                @block.sync
                def _(s):
                    C["CONSTS"].wait(s, 1)
                    C["KAUX"].inc(s.dma_start(out=k_sb[64:70, :], in_=kaux[:, :]))
                    C["KAUX"].inc(s.dma_start(out=mask_sb[:, :], in_=mask4[:, :]))
                    C["KAUX"].inc(s.dma_start(out=q_dram[64:KROW, :], in_=qaux2[:, :]))
                    for tb in range(NTB):
                        if tb >= 2:
                            C["QST"].wait(s, tb - 2)
                        C["COS"].inc(s.dma_start(out=cos_t[:, (tb % 2) * 512:(tb % 2 + 1) * 512],
                                                 in_=cos128[:, tb * 512:(tb + 1) * 512]))
                        C["COS"].inc(s.dma_start(out=sin_t[:, (tb % 2) * 512:(tb % 2 + 1) * 512],
                                                 in_=sin128[:, tb * 512:(tb + 1) * 512]))
                        if tb >= 1:
                            _qd_store(s, tb - 1)
                    _qd_store(s, NTB - 1)

                @block.tensor
                def _(t):
                    C["WQKV"].wait(t, 2)
                    for tb in range(NTB):
                        b = tb % 2
                        C["P1X"].wait(t, tb)
                        if tb >= 2:
                            C["KR"].wait(t, tb - 2)
                            C["VE"].wait(t, tb - 2)
                            C["QST"].wait(t, tb - 2)
                        xt = x_tile[:, b * S:(b + 1) * S]
                        for dc in range(NDC):
                            i_ = t.matmul(kps(b), wk_sb[:, dc * 64:(dc + 1) * 64],
                                          xt[:, dc * 512:(dc + 1) * 512],
                                          start=(dc == 0), stop=(dc == NDC - 1))
                        C["KM"].inc(i_)
                        for ts4 in range(4):
                            for dc in range(NDC):
                                i_ = t.matmul(vps(b)[:, ts4 * 64:(ts4 + 1) * 64],
                                              xt[:, dc * 512 + ts4 * 128:dc * 512 + (ts4 + 1) * 128],
                                              wv_sb[:, dc * 64:(dc + 1) * 64],
                                              start=(dc == 0), stop=(dc == NDC - 1))
                        C["VM"].inc(i_)
                        for dc in range(NDC):
                            t.matmul(qevp(b), wq_sb[:, dc * 256:dc * 256 + 128],
                                     xt[:, dc * 512:(dc + 1) * 512],
                                     start=(dc == 0), stop=(dc == NDC - 1))
                        for dc in range(NDC):
                            i_ = t.matmul(qodp(b), wq_sb[:, dc * 256 + 128:dc * 256 + 256],
                                          xt[:, dc * 512:(dc + 1) * 512],
                                          start=(dc == 0), stop=(dc == NDC - 1))
                        C["QM"].inc(i_)

                @block.vector
                def _(v):
                    for tb in range(NTB):
                        b = tb % 2
                        cosv = cos_t[:, b * 512:(b + 1) * 512]
                        sinv = sin_t[:, b * 512:(b + 1) * 512]
                        tc = tb * 512
                        C["COS"].wait(v, 2 * tb + 1)
                        C["KM"].wait(v, tb)
                        v.tensor_copy(out=qrs[0:1, 0:1], in_=consts[0:1, 0:1])
                        ke, ko = kps(b)[0:32, :], kps(b)[32:64, :]
                        a1, a2 = krs[:, 0:512], krs[:, 512:1024]
                        v.tensor_tensor(out=a1, in0=ke, in1=cosv[0:32, :], op=ALU.mult)
                        v.tensor_tensor(out=a2, in0=ko, in1=sinv[0:32, :], op=ALU.mult)
                        v.tensor_tensor(out=k_sb[0:32, tc:tc + 512], in0=a1, in1=a2, op=ALU.subtract)
                        v.tensor_tensor(out=a1, in0=ko, in1=cosv[0:32, :], op=ALU.mult)
                        v.tensor_tensor(out=a2, in0=ke, in1=sinv[0:32, :], op=ALU.mult)
                        C["KR"].inc(v.tensor_tensor(out=k_sb[32:64, tc:tc + 512],
                                                    in0=a1, in1=a2, op=ALU.add))
                        C["VM"].wait(v, tb)
                        v.tensor_copy(out=qrs[0:1, 0:1], in_=consts[0:1, 0:1])
                        C["VE"].inc(v.tensor_copy(
                            out=v_sb[:].rearrange("p (k c) -> p k c", c=65)[:, 4 * tb:4 * tb + 4, 0:64],
                            in_=vps(b)[:].rearrange("p (k c) -> p k c", c=64)))
                        C["QM"].wait(v, tb)
                        if tb >= 2:
                            C["QD"].wait(v, tb - 2)
                        b1, b2 = qrs[:, 0:512], qrs[:, 512:1024]
                        qsg = qstage[:, b * 2048:(b + 1) * 2048]
                        v.tensor_tensor(out=b1, in0=qevp(b), in1=cosv, op=ALU.mult)
                        v.tensor_tensor(out=b2, in0=qodp(b), in1=sinv, op=ALU.mult)
                        for hh in range(HPC):
                            v.tensor_tensor(out=qsg[0:32, hh * 512:(hh + 1) * 512],
                                            in0=b1[32 * hh:32 * hh + 32, :],
                                            in1=b2[32 * hh:32 * hh + 32, :], op=ALU.subtract)
                        v.tensor_tensor(out=b1, in0=qodp(b), in1=cosv, op=ALU.mult)
                        v.tensor_tensor(out=b2, in0=qevp(b), in1=sinv, op=ALU.mult)
                        last = None
                        for hh in range(HPC):
                            last = v.tensor_tensor(out=qsg[32:64, hh * 512:(hh + 1) * 512],
                                                   in0=b1[32 * hh:32 * hh + 32, :],
                                                   in1=b2[32 * hh:32 * hh + 32, :], op=ALU.add)
                        C["QST"].inc(last)


                if debug == "p1":
                    @block.gpsimd
                    def _(g):
                        C["KR"].wait(g, NTB - 1)
                        C["VE"].wait(g, NTB - 1)
                        C["QD"].wait(g, NTB - 1)
                        C["KAUX"].wait(g, 2)
                        C["ST"].inc(g.dma_start(out=dbg["k"], in_=k_sb[0:KROW, :]))
                        C["ST"].inc(g.dma_start(out=dbg["v"], in_=v_sb[:, :]))
                        C["ST"].inc(g.dma_start(out=dbg["q"], in_=q_dram[:, :]))
                        C["ST"].wait(g, 2)
                    es1.close()
                    return nc

                evidx = lambda qc, hh: 4 * qc + hh

                def _mk_idx_of(qc, kb, delta):
                    kk = kb + delta
                    if kk < 0:
                        if qc == 0:
                            return None
                        qq, kk2 = qc - 1, 4 * (qc - 1) + 4 + kk
                        if kk2 >= 4 * qq:
                            return 4 * qq + (kk2 - 4 * qq)
                        return None
                    if kk >= 4 * qc:
                        return 4 * qc + (kk - 4 * qc)
                    return None
                no_evac = debug == "attn0"

                def _cs_store(s, qc):
                    if no_evac:
                        return
                    for hh in range(HPC):
                        sl = (qc % 2) * 4 + hh
                        C["CM"].wait(s, evidx(qc, hh))
                        C["CS"].inc(s.dma_start(
                            out=ctx_loc[hh * 64:(hh + 1) * 64, qc * 512:(qc + 1) * 512],
                            in_=cstage[:, sl * 512:(sl + 1) * 512]))

                @block.sync
                def _(s):
                    C["KAUX"].wait(s, 2)
                    for qc in range(NTB):
                        if qc >= 2:
                            C["CT"].wait(s, kbidx(qc - 2, 4 * (qc - 2) + 3))
                        C["QD"].wait(s, qc)
                        C["QSB"].inc(s.dma_start(
                            out=q_sb[0:KROW, (qc % 2) * 2048:(qc % 2 + 1) * 2048].rearrange(
                                "r (h t) -> r h t", h=HPC),
                            in_=q_dram.rearrange("r (h t) -> r h t", h=HPC)[:, :, qc * 512:(qc + 1) * 512]))
                        if qc >= 1:
                            _cs_store(s, qc - 1)
                    _cs_store(s, NTB - 1)

                @block.tensor
                def _(t):
                    C["KR"].wait(t, NTB - 1)
                    C["VE"].wait(t, NTB - 1)
                    C["QST"].wait(t, NTB - 1)
                    C["KAUX"].wait(t, 0)
                    C["CONSTS"].wait(t, 1)
                    for qc in range(NTB):
                        nkb = 4 * qc + 4
                        C["QSB"].wait(t, qc)
                        if qc == 0:
                            t.nop(cycle_cnt=4000)
                            t.nop(cycle_cnt=4000)
                        if qc >= 1:
                            C["CM"].wait(t, evidx(qc - 1, HPC - 1))
                            if not no_evac:
                                C["BE"].wait(t, evidx(qc - 1, HPC - 1))

                        def scores(kb):
                            g = kbidx(qc, kb)
                            if g >= 1:
                                C["TH"].wait(t, g - 1)
                            for hh in range(HPC):
                                i_ = t.matmul(ps_a[:, hh * 512:(hh + 1) * 512],
                                              k_sb[:, kb * 128:(kb + 1) * 128],
                                              q_sb[:, (qc % 2) * 2048 + hh * 512:(qc % 2) * 2048 + (hh + 1) * 512],
                                              start=True, stop=True)
                            C["SC"].inc(i_)

                        def ctxm(kb):
                            g = kbidx(qc, kb)
                            C["EX"].wait(t, g)
                            t.nop(cycle_cnt=400)
                            src = probs_r
                            for hh in range(HPC):
                                i_ = t.matmul(ps_b[0:65, hh * 512:(hh + 1) * 512],
                                              v_sb[:, kb * 65:(kb + 1) * 65],
                                              src[:, (kb % 4) * 2048 + hh * 512:(kb % 4) * 2048 + (hh + 1) * 512],
                                              start=(kb == 0), stop=(kb == nkb - 1))
                            C["CT"].inc(i_)

                        for kb in range(nkb):
                            scores(kb)
                            if kb >= 1:
                                ctxm(kb - 1)
                        ctxm(nkb - 1)
                        if no_evac:
                            continue
                        C["TH"].wait(t, kbidx(qc, nkb - 1))
                        C["RC"].wait(t, 2 * qc + 1)
                        t.nop(cycle_cnt=400)
                        for hh in range(HPC):
                            if hh >= 1:
                                C["BE"].wait(t, evidx(qc, hh - 1))
                            C["BC"].inc(t.matmul(ps_a[0:64, 0:512], ones65_bf[0:33, 0:64],
                                                 dr3[:, hh * 512:(hh + 1) * 512],
                                                 start=True, stop=True))

                @block.scalar
                def _(a):
                    for qc in range(NTB):
                        nkb = 4 * qc + 4
                        for kb in range(nkb):
                            g = kbidx(qc, kb)
                            C["SC"].wait(a, g)
                            if kb == 0:
                                a.copy(out=consts[:, 3:4], in_=consts[:, 0:1])
                            if debug == "attn1" and qc == 0 and kb == 0:
                                a.copy(out=qsb_keep[:, :], in_=ps_a[:, 0:2048])
                            mk_prev = _mk_idx_of(qc, kb, -2)
                            if mk_prev is not None:
                                C["MK"].wait(a, mk_prev)
                            C["TH"].inc(a.activation(
                                out=tanh_r[:, (kb % 2) * 2048:(kb % 2 + 1) * 2048],
                                in_=ps_a[:, 0:2048], func=AF.Tanh, scale=1.0 / 240.0))
                            if g >= 4:
                                C["CT"].wait(a, g - 4)
                            if kb >= 4 * qc:
                                C["MK"].wait(a, 4 * qc + (kb - 4 * qc))
                                C["EX"].inc(a.activation(
                                    out=probs_r[:, (kb % 4) * 2048:(kb % 4 + 1) * 2048],
                                    in_=tm_r[:, (kb % 2) * 2048:(kb % 2 + 1) * 2048],
                                    func=AF.Exp, scale=30.0, bias=-60.0))
                            else:
                                C["EX"].inc(a.activation(
                                    out=probs_r[:, (kb % 4) * 2048:(kb % 4 + 1) * 2048],
                                    in_=tanh_r[:, (kb % 2) * 2048:(kb % 2 + 1) * 2048],
                                    func=AF.Exp, scale=30.0, bias=-30.0))

                @block.vector
                def _(v):
                    C["KAUX"].wait(v, 1)
                    for qc in range(NTB):
                        nkb = 4 * qc + 4
                        for kb in range(nkb):
                            g = kbidx(qc, kb)
                            if kb >= 4 * qc:
                                aoff = (kb - 4 * qc) * 512
                                C["TH"].wait(v, g)
                                if g >= 2:
                                    C["EX"].wait(v, g - 2)
                                v.tensor_copy(out=qrs[0:1, 0:1], in_=consts[0:1, 0:1])
                                th = tanh_r[:, (kb % 2) * 2048:(kb % 2 + 1) * 2048]
                                tm = tm_r[:, (kb % 2) * 2048:(kb % 2 + 1) * 2048]
                                C["MK"].inc(v.scalar_tensor_tensor(
                                    out=tm.rearrange("p (h t) -> p h t", h=HPC),
                                    in0=th.rearrange("p (h t) -> p h t", h=HPC),
                                    scalar=1.0,
                                    in1=mask_sb[:, aoff:aoff + 512].unsqueeze(1).broadcast_to([128, HPC, 512]),
                                    op0=ALU.add, op1=ALU.mult))

                        C["CT"].wait(v, kbidx(qc, nkb - 1))
                        if no_evac:
                            for hh in range(HPC):
                                sl = (qc % 2) * 4 + hh
                                C["CM"].inc(v.tensor_copy(
                                    out=cstage[:, sl * 512:(sl + 1) * 512],
                                    in_=ps_b[0:65, hh * 512:(hh + 1) * 512][0:64, :]))
                            continue
                        v.memset(qrs[0:1, 0:512], 0.0)
                        v.memset(qrs[0:1, 0:512], 0.0)
                        C["DC"].inc(v.tensor_copy(out=den_s[:, :], in_=ps_b[64:65, 0:2048]))
                        if debug == "attn1" and qc == 0:
                            v.tensor_copy(out=den_keep[:, :], in_=den_s[:, :])
                            v.tensor_copy(out=ps_keep[:, :], in_=ps_b[0:65, 0:2048])

                        if qc >= 1:
                            C["BC"].wait(v, evidx(qc - 1, HPC - 1))
                        v.reciprocal(out=denr_all[:, :], in_=den_s[:, :])
                        v.tensor_copy(out=dr3[0:1, :], in_=denr_all[:, :])
                        v.tensor_copy(out=drs[:, :], in_=dr3[0:1, :])
                        C["RC"].inc(v.tensor_tensor(out=den_s[:, :], in0=denr_all[:, :], in1=drs[:, :], op=ALU.subtract))
                        C["RC"].inc(v.tensor_copy(out=dr3[32:33, :], in_=den_s[:, :]))
                        for hh in range(HPC):
                            e = evidx(qc, hh)
                            sl = (qc % 2) * 4 + hh
                            C["BC"].wait(v, e)
                            v.memset(qrs[0:1, 0:512], 0.0)
                            C["BE"].inc(v.tensor_copy(out=den_bc[:, :], in_=ps_a[0:64, 0:512]))
                            if e >= 8:
                                C["CS"].wait(v, e - 8)
                            C["CM"].inc(v.tensor_tensor(
                                out=cstage[:, sl * 512:(sl + 1) * 512],
                                in0=ps_b[0:64, hh * 512:(hh + 1) * 512],
                                in1=den_bc[:, :], op=ALU.mult))


                if debug == "attn0":
                    @block.gpsimd
                    def _(g):
                        C["CM"].wait(g, evidx(NTB - 1, HPC - 1))
                        C["ST"].inc(g.dma_start(out=dbg["ps"][0:64, :],
                                                in_=cstage[:, 0:4 * 512]))
                        C["ST"].wait(g, 0)
                    es1.close()
                    return nc

                if debug == "attn1":
                    @block.gpsimd
                    def _(g):
                        C["QSB"].wait(g, 0)
                        C["ST"].inc(g.dma_start(out=dbg["qsbraw"], in_=q_sb[0:KROW, 0:2048]))
                        C["ST"].inc(g.dma_start(out=dbg["k"], in_=k_sb[0:KROW, :]))
                        C["CS"].wait(g, evidx(NTB - 1, HPC - 1))
                        C["ST"].inc(g.dma_start(
                            out=dbg["qdr"].rearrange("r (h t) -> r h t", h=HPC),
                            in_=q_dram.rearrange("r (h t) -> r h t", h=HPC)[:, :, 0:512]))
                        C["ST"].inc(g.dma_start(out=dbg["ctxl"], in_=ctx_loc[:, :]))
                        C["ST"].inc(g.dma_start(out=dbg["den0"], in_=den_keep[:, :]))
                        C["ST"].inc(g.dma_start(out=dbg["ps0"], in_=ps_keep[:, :]))
                        C["ST"].inc(g.dma_start(out=dbg["qsb0"], in_=qsb_keep[:, :]))
                        C["ST"].wait(g, 6)
                    es1.close()
                    return nc

                @block.gpsimd
                def _(g):
                    for oc in range(2):
                        C["WODMA"].inc(g.dma_start(
                            out=wo_sb[:, (oc % 2) * 2048:(oc % 2 + 1) * 2048].rearrange(
                                "p (c n) -> p c n", n=128),
                            in_=wo.rearrange("(c p) n -> p c n", p=128)[:, :, oc * 128:(oc + 1) * 128]))
                    C["CS"].wait(g, evidx(NTB - 1, HPC - 1))
                    C["CC"].inc(g.collective_compute(
                        "AllGather", ALU.bypass,
                        replica_groups=[list(range(W))],
                        ins=[ctx_loc.opt()],
                        outs=[ctx_full.opt()]))
                    C["CC"].wait(g, 0)
                    C["CC"].inc(g.collective_compute(
                        "AllGather", ALU.bypass,
                        replica_groups=[list(range(W))],
                        ins=[flag_in.opt()],
                        outs=[flag_out.opt()]))

                if debug == "attn":
                    @block.gpsimd
                    def _(g):
                        C["CC"].wait(g, 1)
                        C["ST"].inc(g.dma_start(out=dbg["ctx"], in_=ctx_full[:, :]))
                        C["ST"].wait(g, 0)
                    es1.close()
                    return nc

            es1.close()
            es2 = _ctl.ExitStack()
            rbig = es2.enter_context(nc.sbuf_tensor("rbig", [128, NDC * 512], F32))
            x2 = es2.enter_context(nc.sbuf_tensor("x2", [128, NDC * 512], F32))
            x2bf = es2.enter_context(nc.sbuf_tensor("x2bf", [128, NDC * 512], BF16))
            statbf = es2.enter_context(nc.sbuf_tensor("statbf", [128, 2 * 512], BF16))
            sqbf = es2.enter_context(nc.sbuf_tensor("sqbf", [128, 2 * 512], BF16))
            rows = es2.enter_context(nc.sbuf_tensor("rows", [1, 2560], F32))
            mbc = es2.enter_context(nc.sbuf_tensor("mbc", [128, 512], F32))
            sbc = es2.enter_context(nc.sbuf_tensor("sbc", [128, 512], F32))
            x3st = es2.enter_context(nc.sbuf_tensor("x3st", [128, 2 * 512], F32))
            lnsplit = es2.enter_context(nc.sbuf_tensor("lnsplit", [65, 1024], BF16))
            lnrs = es2.enter_context(nc.sbuf_tensor("lnrs", [1, 1024], F32))
            if True:

                env = dict(nc=nc, block=block, C=C, ps_a=ps_a, ps_b=ps_b,
                           rbig=rbig, x2=x2, x2bf=x2bf, statbf=statbf, sqbf=sqbf,
                           rows=rows, mbc=mbc, sbc=sbc, x3st=x3st, consts=consts,
                           ones1=ones1, onescol_bf=onescol_bf, ones65_bf=ones65_bf,
                           lnsplit=lnsplit, lnrs=lnrs, bo_sb=bo_sb, wo_sb=wo_sb, dbg=dbg, debug=debug,
                           ctx_full=ctx_full, x_dense=x_dense, wo=wo, wg_T=wg_T,
                           wu_T=wu_T, wd_T=wd_T, out_fm=out_fm)
                @block.vector
                def _(v):
                    v.memset(lnsplit[:, :], 0.0)

                for tt in range(2):
                    _dense_pass(env, tt)

                @block.sync
                def _(s):
                    C["ST"].wait(s, 2 * NDC - 1)
                es2.close()

    return nc


def _dense_pass(env, tt):
    nc, block, C = env["nc"], env["block"], env["C"]
    ps_a, ps_b = env["ps_a"], env["ps_b"]
    rbig, x2, x2bf = env["rbig"], env["x2"], env["x2bf"]
    bo_sb, ones1 = env["bo_sb"], env["ones1"]
    lnrs, consts = env["lnrs"], env["consts"]
    ctx_full, x_dense = env["ctx_full"], env["x_dense"]
    wo, wg_T, wu_T, wd_T, out_fm = env["wo"], env["wg_T"], env["wu_T"], env["wd_T"], env["out_fm"]
    tok0 = tt * 512
    ocg = lambda oc: tt * NDC + oc
    hcg = lambda hc: tt * NHC + hc

    import contextlib as _ctl
    esp = _ctl.ExitStack()
    ctxf = esp.enter_context(nc.sbuf_tensor(f"ctxf{tt}", [128, NDC * 512], BF16))
    wo_sb = env["wo_sb"]
    xres = esp.enter_context(nc.sbuf_tensor(f"xres{tt}", [128, 2 * 512], F32))
    if True:

        @block.sync
        def _(s):
            pid = nc.partition_id(engines=[mybir.EngineType.SP])
            C["CC"].wait(s, 1)
            C["CTXF"].inc(s.dma_start(
                out=ctxf[:].rearrange("p (c t) -> p c t", t=512),
                in_=ctx_full.rearrange("(c p) t -> p c t", p=128)[:, :, bass.ds(pid * TPC + tok0, 512)]))
            for oc in range(NDC):
                if ocg(oc) >= 2:
                    C["REP"].wait(s, ocg(oc) - 2)
                C["XRES"].inc(s.dma_start(
                    out=xres[:, (oc % 2) * 512:(oc % 2 + 1) * 512],
                    in_=x_dense[oc * 128:(oc + 1) * 128, tok0:tok0 + 512]))

        @block.gpsimd
        def _(g):
            for oc in range(2 if tt == 0 else 0, NDC):
                if ocg(oc) >= 2:
                    C["WOM"].wait(g, ocg(oc) - 2)
                C["WODMA"].inc(g.dma_start(
                    out=wo_sb[:, (oc % 2) * 2048:(oc % 2 + 1) * 2048].rearrange(
                        "p (c n) -> p c n", n=128),
                    in_=wo.rearrange("(c p) n -> p c n", p=128)[:, :, oc * 128:(oc + 1) * 128]))
            if env["debug"] == "d0" and tt == 0:
                dbg = env["dbg"]
                C["CTXF"].wait(g, 0)
                C["DBG"].inc(g.dma_start(out=dbg["ctxf"], in_=ctxf[:, :]))
                C["REP"].wait(g, NDC - 1)
                C["DBG"].inc(g.dma_start(out=dbg["rb"], in_=rbig[:, :]))
                C["APP"].wait(g, NDC - 1)
                C["DBG"].inc(g.dma_start(out=dbg["x2"], in_=x2[:, :]))

        @block.tensor
        def _(t):
            C["CTXF"].wait(t, tt)
            t.nop(cycle_cnt=4000)
            if tt >= 1:
                C["R2"].wait(t, tt * NDC - 1)
            for oc in range(NDC):
                C["WODMA"].wait(t, ocg(oc))
                if oc == 0:
                    t.nop(cycle_cnt=2000)
                if ocg(oc) >= 2:
                    C["REP"].wait(t, ocg(oc) - 2)
                for cc in range(NDC):
                    i_ = t.matmul(ps_a[:, (oc % 2) * 512:(oc % 2 + 1) * 512],
                                  wo_sb[:, (oc % 2) * 2048 + cc * 128:(oc % 2) * 2048 + (cc + 1) * 128],
                                  ctxf[:, cc * 512:(cc + 1) * 512],
                                  start=(cc == 0), stop=(cc == NDC - 1))
                C["WOM"].inc(i_)

        @block.vector
        def _(v):
            C["BO"].wait(v, 0)
            for oc in range(NDC):
                C["WOM"].wait(v, ocg(oc))
                C["XRES"].wait(v, ocg(oc))
                v.tensor_copy(out=lnrs[:, 0:1], in_=consts[0:1, 0:1])
                C["REP"].inc(v.scalar_tensor_tensor(
                    out=rbig[:, oc * 512:(oc + 1) * 512],
                    in0=ps_a[:, (oc % 2) * 512:(oc % 2 + 1) * 512],
                    scalar=bo_sb[:, oc:oc + 1],
                    in1=xres[:, (oc % 2) * 512:(oc % 2 + 1) * 512],
                    op0=ALU.add, op1=ALU.add))

        _layernorm(env, ln_i=2 * tt, src=rbig, dst=x2, dstbf=x2bf,
                   dep="REP", dep_base=tt * NDC, out_dma=None)

        esp.close()
        esf = _ctl.ExitStack()
        hidden = esf.enter_context(nc.sbuf_tensor(f"hidden{tt}", [128, NHC * 512], BF16))
        wg_sb = esf.enter_context(nc.sbuf_tensor(f"wg_sb{tt}", [128, 2 * 2048], BF16))
        wu_sb = esf.enter_context(nc.sbuf_tensor(f"wu_sb{tt}", [128, 2 * 2048], BF16))
        silu_sb = esf.enter_context(nc.sbuf_tensor(f"silu{tt}", [128, 2 * 512], F32))
        wd_sb = esf.enter_context(nc.sbuf_tensor(f"wd_sb{tt}", [128, 2 * 1024], BF16))
        if True:

            @block.gpsimd
            def _(g):
                for hc in range(NHC):
                    if hcg(hc) >= 2:
                        C["GU"].wait(g, hcg(hc) - 2)
                    C["WG"].inc(g.dma_start(
                        out=wg_sb[:, (hc % 2) * 2048:(hc % 2 + 1) * 2048].rearrange(
                            "p (c n) -> p c n", n=128),
                        in_=wg_T.rearrange("(c p) n -> p c n", p=128)[:, :, hc * 128:(hc + 1) * 128]))
                    C["WG"].inc(g.dma_start(
                        out=wu_sb[:, (hc % 2) * 2048:(hc % 2 + 1) * 2048].rearrange(
                            "p (c n) -> p c n", n=128),
                        in_=wu_T.rearrange("(c p) n -> p c n", p=128)[:, :, hc * 128:(hc + 1) * 128]))

            @block.tensor
            def _(t):
                C["APP"].wait(t, 2 * tt * NDC + NDC - 1)
                t.nop(cycle_cnt=400)
                for hc in range(NHC):
                    C["WG"].wait(t, 2 * hcg(hc) + 1)
                    if hc == 0:
                        t.nop(cycle_cnt=2000)
                    if hcg(hc) >= 2:
                        C["SI"].wait(t, hcg(hc) - 2)
                        C["HD"].wait(t, hcg(hc) - 2)
                    for dc in range(NDC):
                        t.matmul(ps_a[:, (hc % 2) * 512:(hc % 2 + 1) * 512],
                                 wg_sb[:, (hc % 2) * 2048 + dc * 128:(hc % 2) * 2048 + (dc + 1) * 128],
                                 x2bf[:, dc * 512:(dc + 1) * 512],
                                 start=(dc == 0), stop=(dc == NDC - 1))
                    for dc in range(NDC):
                        i_ = t.matmul(ps_a[:, 1024 + (hc % 2) * 512:1024 + (hc % 2 + 1) * 512],
                                      wu_sb[:, (hc % 2) * 2048 + dc * 128:(hc % 2) * 2048 + (dc + 1) * 128],
                                      x2bf[:, dc * 512:(dc + 1) * 512],
                                      start=(dc == 0), stop=(dc == NDC - 1))
                    C["GU"].inc(i_)

            @block.scalar
            def _(a):
                for hc in range(NHC):
                    C["GU"].wait(a, hcg(hc))
                    a.copy(out=consts[:, 3:4], in_=consts[:, 0:1])
                    C["SI"].inc(a.activation(
                        out=silu_sb[:, (hc % 2) * 512:(hc % 2 + 1) * 512],
                        in_=ps_a[:, (hc % 2) * 512:(hc % 2 + 1) * 512], func=AF.Silu))

            @block.vector
            def _(v):
                for hc in range(NHC):
                    C["SI"].wait(v, hcg(hc))
                    v.tensor_copy(out=lnrs[:, 0:1], in_=consts[0:1, 0:1])
                    C["HD"].inc(v.tensor_tensor(
                        out=hidden[:, hc * 512:(hc + 1) * 512],
                        in0=ps_a[:, 1024 + (hc % 2) * 512:1024 + (hc % 2 + 1) * 512],
                        in1=silu_sb[:, (hc % 2) * 512:(hc % 2 + 1) * 512],
                        op=ALU.mult))

            dng = lambda dg, hc: (2 * tt + dg) * NHC + hc

            @block.gpsimd
            def _(g):
                if env["debug"] == "d0" and tt == 0:
                    dbg = env["dbg"]
                    C["HD"].wait(g, NHC - 1)
                    C["DBG"].inc(g.dma_start(out=dbg["hid"], in_=hidden[:, :]))
                for dg in range(2):
                    for hc in range(NHC):
                        if dng(dg, hc) >= 2:
                            C["DN"].wait(g, dng(dg, hc) - 2)
                        C["WD"].inc(g.dma_start(
                            out=wd_sb[:, (hc % 2) * 1024:(hc % 2 + 1) * 1024],
                            in_=wd_T[hc * 128:(hc + 1) * 128, dg * 1024:(dg + 1) * 1024]))

            @block.tensor
            def _(t):
                C["HD"].wait(t, hcg(NHC - 1))
                t.nop(cycle_cnt=400)
                for dg in range(2):
                    if dg == 1 or tt == 1:
                        C["R2"].wait(t, tt * NDC + dg * 8 - 1 if dg == 1 else (tt - 1) * NDC + NDC - 1)
                    for hc in range(NHC):
                        C["WD"].wait(t, dng(dg, hc))
                        if hc == 0:
                            t.nop(cycle_cnt=2000)
                        for dci in range(8):
                            psd = (ps_a if dci < 4 else ps_b)[:, (dci % 4) * 512:(dci % 4 + 1) * 512]
                            i_ = t.matmul(psd,
                                          wd_sb[:, (hc % 2) * 1024 + dci * 128:(hc % 2) * 1024 + (dci + 1) * 128],
                                          hidden[:, hc * 512:(hc + 1) * 512],
                                          start=(hc == 0), stop=(hc == NHC - 1))
                        C["DN"].inc(i_)

            @block.vector
            def _(v):
                for dg in range(2):
                    C["DN"].wait(v, dng(dg, NHC - 1))
                    v.tensor_copy(out=lnrs[:, 0:1], in_=consts[0:1, 0:1])
                    for dci in range(8):
                        dc = dg * 8 + dci
                        psd = (ps_a if dci < 4 else ps_b)[:, (dci % 4) * 512:(dci % 4 + 1) * 512]
                        C["R2"].inc(v.tensor_tensor(
                            out=rbig[:, dc * 512:(dc + 1) * 512],
                            in0=psd, in1=x2[:, dc * 512:(dc + 1) * 512], op=ALU.add))

        if env["debug"] == "d0" and tt == 0:
            @block.gpsimd
            def _(g):
                dbg = env["dbg"]
                C["R2"].wait(g, NDC - 1)
                C["DBG"].inc(g.dma_start(out=dbg["r2"], in_=rbig[:, :]))
                C["DBG"].wait(g, 4)
        esf.close()
        _layernorm(env, ln_i=2 * tt + 1, src=rbig, dst=None, dstbf=None,
                   dep="R2", dep_base=tt * NDC, out_dma=(out_fm, tok0, tt))


def _layernorm(env, ln_i, src, dst, dstbf, dep, dep_base, out_dma):
    nc, block, C = env["nc"], env["block"], env["C"]
    ps_a, ps_b = env["ps_a"], env["ps_b"]
    statbf, sqbf, rows = env["statbf"], env["sqbf"], env["rows"]
    mbc, sbc, x3st = env["mbc"], env["sbc"], env["x3st"]
    ones1, onescol_bf = env["ones1"], env["onescol_bf"]

    dci_g = lambda dc: ln_i * NDC + dc
    psum_sum = ps_b[0:1, 0:512]
    psum_sq = ps_b[0:1, 512:1024]
    psum_mb = ps_b[:, 1024:1536]
    psum_sb = ps_b[:, 1536:2048]
    mrow = rows[:, 0:512]
    m2row = rows[:, 512:1024]
    varrow = rows[:, 1024:1536]
    sdrow = rows[:, 1536:2048]
    rstd = rows[:, 2048:2560]
    lsp = env["lnsplit"]
    lrs = env["lnrs"]
    consts = env["consts"]
    rbase_dve = 2 * ln_i
    rbase_act = 2 * ln_i
    rbase_dve = 3 * ln_i

    @block.vector
    def _(v):
        for dc in range(NDC):
            C[dep].wait(v, dep_base + dc)
            if dci_g(dc) >= 2:
                C["LNM"].wait(v, dci_g(dc) - 2)
            C["LNC"].inc(v.tensor_copy(out=statbf[:, (dc % 2) * 512:(dc % 2 + 1) * 512],
                                       in_=src[:, dc * 512:(dc + 1) * 512]))

    @block.scalar
    def _(a):
        for dc in range(NDC):
            C[dep].wait(a, dep_base + dc)
            if dci_g(dc) >= 2:
                C["LNM"].wait(a, dci_g(dc) - 2)
            C["LNQ"].inc(a.activation(
                out=sqbf[:, (dc % 2) * 512:(dc % 2 + 1) * 512],
                in_=src[:, dc * 512:(dc + 1) * 512], func=AF.Square))

    @block.tensor
    def _(t):
        if ln_i >= 1:
            C["LNE"].wait(t, ln_i - 1)
        for dc in range(NDC):
            C["LNC"].wait(t, dci_g(dc))
            C["LNQ"].wait(t, dci_g(dc))
            t.nop(cycle_cnt=200)
            t.matmul(psum_sum, onescol_bf[:, :], statbf[:, (dc % 2) * 512:(dc % 2 + 1) * 512],
                     start=(dc == 0), stop=(dc == NDC - 1))
            i_ = t.matmul(psum_sq, onescol_bf[:, :], sqbf[:, (dc % 2) * 512:(dc % 2 + 1) * 512],
                          start=(dc == 0), stop=(dc == NDC - 1))
            C["LNM"].inc(i_)

    @block.vector
    def _(v):
        C["LNM"].wait(v, ln_i * NDC + NDC - 1)
        v.tensor_copy(out=lrs[:, 0:1], in_=consts[0:1, 0:1])
        C["LNR"].inc(v.tensor_scalar(out=mrow, in0=psum_sum, scalar1=1.0 / D,
                                     scalar2=None, op0=ALU.mult))

    @block.vector
    def _(v):
        C["LNR"].wait(v, 3 * ln_i)
        v.tensor_tensor(out=m2row, in0=mrow, in1=mrow, op=ALU.mult)
        C["LNR"].inc(v.scalar_tensor_tensor(out=varrow, in0=psum_sq, scalar=1.0 / D,
                                            in1=m2row, op0=ALU.mult, op1=ALU.subtract))

    @block.scalar
    def _(a):
        C["LNR"].wait(a, 3 * ln_i + 1)
        a.copy(out=consts[:, 3:4], in_=consts[:, 0:1])
        C["LNA"].inc(a.activation(out=sdrow, in_=varrow, func=AF.Sqrt, bias=EPS))

    @block.vector
    def _(v):
        C["LNA"].wait(v, ln_i)
        v.tensor_copy(out=lrs[:, 0:1], in_=consts[0:1, 0:1])
        v.reciprocal(out=rstd, in_=sdrow)
        v.tensor_copy(out=lsp[0:1, 0:512], in_=mrow)
        v.tensor_copy(out=lrs[:, 0:512], in_=lsp[0:1, 0:512])
        v.tensor_tensor(out=m2row, in0=mrow, in1=lrs[:, 0:512], op=ALU.subtract)
        v.tensor_copy(out=lsp[32:33, 0:512], in_=m2row)
        v.tensor_copy(out=lrs[:, 0:512], in_=lsp[32:33, 0:512])
        v.tensor_tensor(out=m2row, in0=m2row, in1=lrs[:, 0:512], op=ALU.subtract)
        v.tensor_copy(out=lsp[64:65, 0:512], in_=m2row)
        v.tensor_copy(out=lsp[0:1, 512:1024], in_=rstd)
        v.tensor_copy(out=lrs[:, 512:1024], in_=lsp[0:1, 512:1024])
        v.tensor_tensor(out=varrow, in0=rstd, in1=lrs[:, 512:1024], op=ALU.subtract)
        v.tensor_copy(out=lsp[32:33, 512:1024], in_=varrow)
        v.tensor_copy(out=lrs[:, 512:1024], in_=lsp[32:33, 512:1024])
        v.tensor_tensor(out=varrow, in0=varrow, in1=lrs[:, 512:1024], op=ALU.subtract)
        C["LNR"].inc(v.tensor_copy(out=lsp[64:65, 512:1024], in_=varrow))

    @block.tensor
    def _(t):
        C["LNR"].wait(t, 3 * ln_i + 2)
        t.nop(cycle_cnt=400)
        ones65_bf = env["ones65_bf"]
        t.matmul(psum_mb, ones65_bf[:, :], lsp[:, 0:512], start=True, stop=True)
        C["LNB"].inc(t.matmul(psum_sb, ones65_bf[:, :], lsp[:, 512:1024], start=True, stop=True))

    @block.scalar
    def _(a):
        C["LNB"].wait(a, ln_i)
        a.copy(out=consts[:, 3:4], in_=consts[:, 0:1])
        a.copy(out=mbc[:, :], in_=psum_mb)
        C["LNE"].inc(a.copy(out=sbc[:, :], in_=psum_sb))

    if out_dma is None:
        @block.vector
        def _(v):
            C["LNE"].wait(v, ln_i)
            v.tensor_copy(out=lrs[:, 0:1], in_=consts[0:1, 0:1])
            for dc in range(NDC):
                v.tensor_tensor(out=src[:, dc * 512:(dc + 1) * 512],
                                in0=src[:, dc * 512:(dc + 1) * 512], in1=mbc[:, :],
                                op=ALU.subtract)
                v.tensor_tensor(out=dst[:, dc * 512:(dc + 1) * 512],
                                in0=src[:, dc * 512:(dc + 1) * 512], in1=sbc[:, :],
                                op=ALU.mult)
                C["APP"].inc(v.tensor_copy(out=dstbf[:, dc * 512:(dc + 1) * 512],
                                           in_=dst[:, dc * 512:(dc + 1) * 512]))
    else:
        dram, tk0, tt = out_dma

        @block.vector
        def _(v):
            C["LNE"].wait(v, ln_i)
            v.tensor_copy(out=lrs[:, 0:1], in_=consts[0:1, 0:1])
            for dc in range(NDC):
                gidx = tt * NDC + dc
                if gidx >= 2:
                    C["ST"].wait(v, gidx - 2)
                v.tensor_tensor(out=src[:, dc * 512:(dc + 1) * 512],
                                in0=src[:, dc * 512:(dc + 1) * 512], in1=mbc[:, :],
                                op=ALU.subtract)
                C["APP"].inc(v.tensor_tensor(
                    out=x3st[:, (dc % 2) * 512:(dc % 2 + 1) * 512],
                    in0=src[:, dc * 512:(dc + 1) * 512], in1=sbc[:, :],
                    op=ALU.mult))

        @block.sync
        def _(s):
            for dc in range(NDC):
                C["APP"].wait(s, ln_i * NDC + dc)
                C["ST"].inc(s.dma_start(
                    out=dram[dc * 128:(dc + 1) * 128, tk0:tk0 + 512],
                    in_=x3st[:, (dc % 2) * 512:(dc % 2 + 1) * 512]))


def kernel(**inputs):
    shared, per_core = host_prep(inputs)
    nc = build(debug=None)
    in_maps = []
    for c in range(W):
        m = dict(shared)
        m.update(per_core[c])
        in_maps.append(m)
    res = run_bass_kernel_spmd(nc, in_maps, core_ids=list(range(W)))
    out = np.empty((S, D), dtype=np.float32)
    for c in range(W):
        out[c * TPC:(c + 1) * TPC, :] = res.results[c]["out_fm"].T
    return out.reshape(1, S, D)


if __name__ == "__main__":
    build(debug="p1")
    print("build p1 ok")
